# version 01 (final)
# baseline (speedup 1.0000x reference)
import contextlib
import numpy as np
import ml_dtypes

import concourse.bass as bass
import concourse.mybir as mybir
from concourse.bass_utils import run_bass_kernel_spmd
from concourse.alu_op_type import AluOpType

F32 = mybir.dt.float32
BF16 = mybir.dt.bfloat16
I32 = mybir.dt.int32
AF = mybir.ActivationFunctionType
AX = mybir.AxisListType
OP = AluOpType

NCORES = 8
S_FULL = 8192
D = 2048
TOK = S_FULL // NCORES
NT = TOK // 128
NDT = D // 128
NCH = NT
NH_A = 8
DQ_A = 128
DV_A = 256
QK_A = 1024
SCALE_A = DQ_A ** -0.5
EPS_A = 1e-6
CAP_A = 15.0
H_FFN = 8192
EPS_FFN = 1e-5
NH_B = 16
HD_B = 128
SCALE_B = HD_B ** -0.5
RANK = 16
LORA_SC = 0.5
LN_EPS = 1e-5
MASK_NEG = -1e9
NEG_BIG = -1e30


class Sem:
    def __init__(self, es, nc, name):
        self.h = es.enter_context(nc.semaphore(name))
        self.n = 0

    def bump(self, by=1):
        self.n += by
        return self.n


class Prog:
    ENGINES = ("sync", "tensor", "vector", "scalar", "gpsimd")

    def __init__(self, nc):
        self.nc = nc
        self.q = {e: [] for e in self.ENGINES}

    def emit(self, eng, fn):
        self.q[eng].append(fn)

    def wait(self, eng, sem, val=None):
        v = sem.n if val is None else val
        if v > 0:
            self.q[eng].append(lambda e, s=sem.h, v=v: e.wait_ge(s, v))

    def finalize(self):
        with self.nc.Block() as block:
            @block.sync
            def _(e):
                for f in self.q["sync"]:
                    f(e)

            @block.tensor
            def _(e):
                for f in self.q["tensor"]:
                    f(e)

            @block.vector
            def _(e):
                for f in self.q["vector"]:
                    f(e)

            @block.scalar
            def _(e):
                for f in self.q["scalar"]:
                    f(e)

            @block.gpsimd
            def _(e):
                for f in self.q["gpsimd"]:
                    f(e)


class Arena:

    def __init__(self, nc, es, words):
        self.t = es.enter_context(nc.sbuf_tensor("ARENA", [128, words], F32)).ap()
        self.words = words
        self.live = {}

    def carve(self, name, off_w, rows, cols, dtype):
        esz = {F32: 4, I32: 4, BF16: 2}[dtype]
        nw = (cols * esz + 3) // 4
        assert off_w % 16 == 0, name
        assert off_w + nw <= self.words, f"{name} overflows arena ({off_w}+{nw}>{self.words})"
        for o, (oo, on) in self.live.items():
            if not (off_w + nw <= oo or oo + on <= off_w):
                raise AssertionError(f"arena overlap: {name} vs {o}")
        self.live[name] = (off_w, nw)
        if dtype == F32:
            return self.t[0:rows, off_w:off_w + nw]
        k = 4 // esz
        return self.t.bitcast(dtype)[0:rows, k * off_w:k * off_w + cols]

    def free(self, *names):
        for n in names:
            self.live.pop(n)

    def free_all(self):
        self.live.clear()


SERIAL_PSUM = False

def build_program(debug_taps=(), upto=99, scan_stage=9):
    nc = bass.Bass("TRN2", target_bir_lowering=False)
    es = contextlib.ExitStack()
    taps = set(debug_taps)

    inp = {}

    def EIN(name, shape, dtype=F32):
        inp[name] = nc.dram_tensor(name, list(shape), dtype, kind="ExternalInput").ap()
        return inp[name]

    EIN("x0", (TOK, D))
    EIN("aWq", (QK_A, D)); EIN("aWk", (QK_A, D)); EIN("aWv", (D, D)); EIN("aWog", (D, D))
    EIN("aWig", (NH_A, D)); EIN("aWfg", (NH_A, D))
    EIN("a_big", (NH_A, 1)); EIN("a_bfg", (NH_A, 1))
    EIN("a_gamma_rep", (128, NH_A * DV_A))
    EIN("aWout", (D, D))
    for p in "qkvo":
        EIN(f"bW{p}", (D, D)); EIN(f"bB{p}", (RANK, D)); EIN(f"bA{p}", (D, RANK))
    EIN("b_bq_cols", (128, NDT)); EIN("b_bk_cols", (128, NDT))
    EIN("b_bv_rep", (128, D)); EIN("b_bo_rep", (128, D))
    EIN("cfc_sl", (D, TOK)); EIN("cpr_sl", (D, TOK))
    EIN("identity", (128, 128), BF16)
    EIN("mask", (128, 128))
    EIN("rankmask", (NCORES, 2))

    y_out = nc.dram_tensor("y", [TOK, D], F32, kind="ExternalOutput").ap()
    dbg = {}

    def DOUT(name, shape, dtype=F32):
        dbg[name] = nc.dram_tensor(name, list(shape), dtype, kind="ExternalOutput").ap()
        return dbg[name]

    def DRAM(name, shape, dtype, shared=False):
        if shared:
            return nc.dram_tensor(name, list(shape), dtype, addr_space="Shared").ap()
        return nc.dram_tensor(name, list(shape), dtype).ap()

    xspA = DRAM("xspA", (TOK, D), F32)
    xspB = DRAM("xspB", (TOK, D), F32)
    xn_hbm = DRAM("xn_hbm", (TOK, D), BF16)
    qta_hbm = DRAM("qta_hbm", (QK_A, TOK), BF16)
    kta_hbm = DRAM("kta_hbm", (QK_A, TOK), BF16)
    va_hbm = DRAM("va_hbm", (TOK, D), BF16)
    sig_hbm = DRAM("sig_hbm", (TOK, D), BF16)
    hout_hbm = DRAM("hout_hbm", (TOK, D), BF16)
    row_hbm = DRAM("row_hbm", (64, 64), F32)
    rA_hbm = DRAM("rA_hbm", (NH_A, NH_A), F32)
    rB_hbm = DRAM("rB_hbm", (NH_A, NH_A), F32)
    rC_hbm = DRAM("rC_hbm", (NH_A, NH_A), F32)
    rM_hbm = DRAM("rM_hbm", (NH_A, 1), F32)
    rMu_hbm = DRAM("rMu_hbm", (NH_A, 1), F32)
    rLam_hbm = DRAM("rLam_hbm", (NH_A, 1), F32)
    z_hbm = DRAM("z_hbm", (TOK, H_FFN), BF16)
    w2n_hbm = DRAM("w2n_hbm", (D, TOK), BF16)
    ao_hbm = DRAM("ao_hbm", (TOK, D), BF16)
    ar_in = DRAM("ar_in", (128, 2), F32); ar_out = DRAM("ar_out", (128, 2), F32)
    w1q_in = DRAM("w1q_in", (D, TOK), BF16); w1q_g = DRAM("w1q_g", (NCORES * D, TOK), BF16, shared=True)
    w2qT_in = DRAM("w2qT_in", (TOK, D), BF16); w2qT_g = DRAM("w2qT_g", (NCORES * TOK, D), BF16, shared=True)
    as_in = DRAM("as_in", (128, NH_A * 257), F32); as_g = DRAM("as_g", (NCORES * 128, NH_A * 257), F32, shared=True)
    am_in = DRAM("am_in", (1, 16), F32); am_g = DRAM("am_g", (NCORES, 16), F32, shared=True)
    ktb_in = DRAM("ktb_in", (D, TOK), BF16); ktb_g = DRAM("ktb_g", (NCORES * D, TOK), BF16, shared=True)
    vb_in = DRAM("vb_in", (TOK, D), BF16); vb_g = DRAM("vb_g", (NCORES * TOK, D), BF16, shared=True)
    nrm_in = DRAM("nrm_in", (1, 2 * NH_B), F32); nrm_g = DRAM("nrm_g", (NCORES, 2 * NH_B), F32, shared=True)

    def SB(name, shape, dtype):
        return es.enter_context(nc.sbuf_tensor(name, list(shape), dtype)).ap()

    stat = SB("stat", (128, 640), F32)
    id_sb = SB("id_sb", (128, 128), BF16)
    mask_sb = SB("mask_sb", (128, 128), F32)
    ones_sb = SB("ones_sb", (128, 128), BF16)
    onesf = SB("onesf", (128, 132), F32)
    onescol_b = SB("onescol_b", (128, 16), BF16)
    misc = SB("misc", (128, 640), F32)
    bcols_sb = SB("bcols_sb", (128, 2 * NDT), F32)
    brep_sb = SB("brep_sb", (128, 2 * D), F32)

    pfull = es.enter_context(nc.psum_tensor("pfull", [128, 4096], F32)).ap()
    banks = [pfull[:, i * 512:(i + 1) * 512] for i in range(8)]

    def bk16(i):
        return pfull.bitcast(BF16)[:, i * 1024:(i + 1) * 1024]

    words = None
    for trial in (47104, 46080, 45056, 44032, 43008, 40960):
        try:
            A = Arena(nc, es, trial)
            words = trial
            break
        except AssertionError:
            continue
    assert words is not None, "arena alloc failed"

    P = Prog(nc)
    SM = {n: Sem(es, nc, n) for n in
          ("ld", "cast", "st", "stg", "tr", "pe", "ve", "se", "cc", "f1", "f2", "f3", "f4", "f5", "f6", "yf")}
    ld, cast, st, stg, tr = SM["ld"], SM["cast"], SM["st"], SM["stg"], SM["tr"]
    pe, ve, se, cc = SM["pe"], SM["ve"], SM["se"], SM["cc"]

    def dma(eng_sem, engq, dst, src):
        s = eng_sem
        s.bump(16)
        P.emit(engq, lambda e, d=dst, sr=src, h=s.h: e.dma_start(out=d, in_=sr).then_inc(h, 16))

    def phase_barrier(queues=("sync", "gpsimd")):
        for q in queues:
            P.wait(q, pe)
            P.wait(q, ve)
            P.wait(q, se)
            P.wait(q, tr)
            P.wait(q, st)

    def tap_barrier():
        for eng in ("sync", "tensor", "vector", "scalar"):
            P.wait(eng, stg)

    def dma_slow(eng_sem, engq, dst, src):
        s = eng_sem
        s.bump(16)
        P.emit(engq, lambda e, d=dst, sr=src, h=s.h: e.dma_start(out=d, in_=sr, allow_slow_non_contiguous=True).then_inc(h, 16))

    def dmaT(dst, src):
        tr.bump(16)
        P.emit("sync", lambda e, d=dst, sr=src, h=tr.h: e.dma_start_transpose(d, sr).then_inc(h, 16))

    def last_inc(eng, sem, fn):
        sem.bump(1)
        P.emit(eng, lambda e, f=fn, h=sem.h: f(e).then_inc(h, 1))

    def vop(fn):
        P.emit("vector", fn)

    def sop(fn):
        P.emit("scalar", fn)

    def vser(fn):
        P.wait("vector", ve)
        last_inc("vector", ve, fn)

    def sser(fn):
        P.wait("scalar", se)
        P.wait("scalar", ve)
        last_inc("scalar", se, fn)

    def top(fn):
        P.emit("tensor", fn)

    def bcast_row(row_ap, n, out_ap, bank_i, pre_pe_wait=None, post=None):
        if pre_pe_wait is not None:
            P.wait("tensor", *pre_pe_wait) if isinstance(pre_pe_wait, tuple) else P.wait("tensor", pre_pe_wait)
        last_inc("tensor", pe, lambda e, r=row_ap, b=banks[bank_i], n=n:
                 e.matmul(b[:, 0:n], onesf[0:1, 0:128], r, start=True, stop=True))
        P.wait("vector", pe)
        last_inc("vector", ve, lambda e, b=banks[bank_i], o=out_ap, n=n: e.tensor_copy(o, b[:, 0:n]))

    _consts = {}

    def constcol(val):
        if val not in _consts:
            k = 8 + len(_consts)
            assert k < 32
            last_inc("vector", ve, lambda e, k=k, v=val: e.memset(misc[:, k:k + 1], v))
            _consts[val] = k
        return misc[:, _consts[val]:_consts[val] + 1]

    dma(ld, "sync", id_sb[:, :], inp["identity"][:, :])
    dma(ld, "sync", mask_sb[:, :], inp["mask"][:, :])
    dma(ld, "sync", bcols_sb[:, 0:NDT], inp["b_bq_cols"][:, :])
    dma(ld, "sync", bcols_sb[:, NDT:2 * NDT], inp["b_bk_cols"][:, :])
    dma(ld, "sync", brep_sb[:, 0:D], inp["b_bv_rep"][:, :])
    dma(ld, "sync", brep_sb[:, D:2 * D], inp["b_bo_rep"][:, :])
    vop(lambda e: e.memset(ones_sb[:, :], 1.0))
    vop(lambda e: e.memset(onesf[:, :], 1.0))
    last_inc("vector", ve, lambda e: e.memset(onescol_b[:, :], 1.0))
    ones_ready = ve.n

    dma(ld, "sync", misc[0:NH_A, 0:1], inp["a_big"][:, :])
    dma(ld, "sync", misc[0:NH_A, 1:2], inp["a_bfg"][:, :])
    biasg_ld = ld.n
    P.wait("vector", ld, biasg_ld)
    last_inc("vector", ve, lambda e: e.tensor_scalar(misc[0:NH_A, 2:4], misc[0:NH_A, 0:2], 1.0 / CAP_A, None, OP.mult))
    biasg_ready = ve.n

    def ph_layernorm(tag, src_dram, xnT, want_xn_hbm=True):
        phase_barrier()
        xr = A.carve(f"xnroll_{tag}", 0, 128, 2 * D, F32)
        sq = A.carve(f"sqdump_{tag}", 4096, 128, D, F32)
        xnst = A.carve(f"xnst_{tag}", 6144, 128, 2 * D, BF16)
        for t in range(NT):
            b = (t % 2) * D
            xt = xr[:, b:b + D]
            if t >= 2:
                P.wait("sync", ve)
            dma(ld, "sync", xt, src_dram[t * 128:(t + 1) * 128, :])
            P.wait("vector", ld)
            last_inc("vector", ve, lambda e, xt=xt, c=stat[:, t:t + 1]: e.reduce_sum(c, xt, axis=AX.X))
            P.wait("scalar", ld)
            last_inc("scalar", se, lambda e, xt=xt, c=stat[:, 16 + t:17 + t]: e.activation(out=sq, in_=xt, func=AF.Square, accum_out=c))
            P.wait("vector", se)
            P.wait("vector", ve)
            vser(lambda e, t=t: e.tensor_scalar(stat[:, 32 + t:33 + t], stat[:, t:t + 1], 1.0 / D, None, OP.mult))
            vser(lambda e, t=t: e.tensor_scalar(stat[:, 40 + t:41 + t], stat[:, 16 + t:17 + t], 1.0 / D, None, OP.mult))
            vser(lambda e, t=t: e.tensor_tensor(stat[:, 48 + t:49 + t], stat[:, 32 + t:33 + t], stat[:, 32 + t:33 + t], OP.mult))
            vser(lambda e, t=t: e.tensor_tensor(stat[:, 56 + t:57 + t], stat[:, 40 + t:41 + t], stat[:, 48 + t:49 + t], OP.subtract))
            _bln = constcol(LN_EPS)
            P.wait("scalar", ve)
            last_inc("scalar", se, lambda e, t=t, b=_bln: e.activation(out=stat[:, 64 + t:65 + t], in_=stat[:, 56 + t:57 + t], func=AF.Sqrt, bias=b))
            P.wait("vector", se)
            last_inc("vector", ve, lambda e, t=t: e.reciprocal(stat[:, 72 + t:73 + t], stat[:, 64 + t:65 + t]))
            P.wait("vector", ve)
            P.wait("vector", st)
            xnt = xnst[:, (t % 2) * D:(t % 2) * D + D]
            last_inc("vector", ve, lambda e, xt=xt, o=xnt, t=t:
                     e.tensor_scalar(o, xt, stat[:, 32 + t:33 + t], stat[:, 72 + t:73 + t], OP.subtract, OP.mult))
            P.wait("sync", ve)
            dma(st, "sync", xn_hbm[t * 128:(t + 1) * 128, :], xnt)
            P.wait("scalar", st)
        P.wait("sync", st)
        for dt in range(NDT):
            dmaT(xnT[:, dt * TOK:(dt + 1) * TOK], xn_hbm[:, dt * 128:(dt + 1) * 128])
        if ("lnstat_" + tag) in taps:
            DOUT(f"dbg_lnstat_{tag}", (128, 128))
            P.wait("gpsimd", ve)
            P.wait("gpsimd", se)
            dma(stg, "gpsimd", dbg[f"dbg_lnstat_{tag}"][:, :], stat[:, 0:128])
            tap_barrier()
        A.free(f"xnroll_{tag}", f"sqdump_{tag}", f"xnst_{tag}")
        if tag in taps:
            DOUT(f"dbg_xnT_{tag}", (128, NDT * TOK), BF16)
            DOUT(f"dbg_xnh_{tag}", (TOK, D), BF16)
            P.wait("gpsimd", tr)
            dma(stg, "gpsimd", dbg[f"dbg_xnT_{tag}"][:, :], xnT)
            dma(stg, "gpsimd", dbg[f"dbg_xnh_{tag}"][:, :], xn_hbm[:, :])
            tap_barrier()

    wcast_hbm = {}

    def cast_weight(name, rows):
        t = DRAM(f"wb_{name}", (rows, D), BF16)
        wcast_hbm[name] = t
        dma(cast, "gpsimd", t[:, :], inp[name][:, :])
        return t, cast.n

    wtst = {"i": 0, "marks": [None, None]}
    evst = {"i": 0, "marks": {}}
    bankmark = {}

    def project(wbf, orows, cast_done, form, evac, xnT, xnT_ready, dt_count=NDT, extra=None):
        nch = orows // 512
        for oci in range(nch):
            bi = wtst["i"] % 2
            wt = wts[bi]
            if wtst["marks"][bi] is not None:
                s, c = wtst["marks"][bi]
                P.wait("sync", s, c)
            P.wait("sync", cast, cast_done)
            for dt in range(dt_count):
                dmaT(wt[:, dt * 512:(dt + 1) * 512], wbf[oci * 512:(oci + 1) * 512, dt * 128:(dt + 1) * 128])
            P.wait("tensor", tr)
            P.wait("tensor", tr, xnT_ready)
            for ot in range(4):
                for th in range(2):
                    bidx = (ot * 2 + th) % 4
                    bank = banks[bidx]
                    if bidx in bankmark:
                        s, c = bankmark[bidx]
                        P.wait("tensor", s, c)
                    for dt in range(dt_count):
                        last = (dt == dt_count - 1) and (extra is None)
                        if form == "B":
                            l = wt[:, dt * 512 + ot * 128: dt * 512 + (ot + 1) * 128]
                            r = xnT[:, dt * TOK + th * 512: dt * TOK + (th + 1) * 512]
                        else:
                            l = xnT[:, dt * TOK + (ot * 2 + th) * 128: dt * TOK + (ot * 2 + th + 1) * 128]
                            r = wt[:, dt * 512: dt * 512 + 512]
                        fn = (lambda e, b=bank, l=l, r=r, s=(dt == 0), p=last:
                              e.matmul(b[:, :], l, r, start=s, stop=p))
                        if last:
                            last_inc("tensor", pe, fn)
                        else:
                            top(fn)
                    if extra is not None:
                        extra(form, bank, oci, ot, th)
                    evac(bank, bidx, oci, ot, th)
            wtst["marks"][bi] = (pe, pe.n)
            wtst["i"] += 1

    def mk_evac(out_hbm, qstage, mode="copy", scale=None, form="B"):
        def evac(bank, bidx, oci, ot, th):
            i = evst["i"]; evst["i"] += 1
            sb = qstage[:, (i % 2) * 512:(i % 2) * 512 + 512]
            eng = "scalar" if mode == "sig" else "vector"
            sem = se if mode == "sig" else ve
            P.wait(eng, pe)
            if i - 2 in evst["marks"]:
                s, c = evst["marks"][i - 2]
                P.wait(eng, s, c)
            if mode == "sig":
                last_inc("scalar", se, lambda e, b=bank, o=sb: e.activation(out=o, in_=b[:, :], func=AF.Sigmoid))
            elif scale is not None:
                last_inc("vector", ve, lambda e, b=bank, o=sb: e.tensor_scalar(o, b[:, :], scale, None, OP.mult))
            else:
                last_inc("vector", ve, lambda e, b=bank, o=sb: e.tensor_copy(o, b[:, :]))
            bankmark[bidx] = (sem, sem.n)
            P.wait("sync", sem)
            if form == "B":
                o0 = oci * 512 + ot * 128
                dma(st, "sync", out_hbm[o0:o0 + 128, th * 512:(th + 1) * 512], sb[0:128, :])
            else:
                tt = ot * 2 + th
                dma(st, "sync", out_hbm[tt * 128:(tt + 1) * 128, oci * 512:(oci + 1) * 512], sb[0:128, :])
            evst["marks"][i] = (st, st.n)
        return evac

    wts = [None, None]

    def ph_attn_a():
        xnT = A.carve("xnT", 8192, 128, NDT * TOK, BF16)
        ph_layernorm("ln1", inp["x0"], xnT)
        xnT_ready = tr.n

        wq_b, cq = cast_weight("aWq", QK_A)
        wk_b, ck = cast_weight("aWk", QK_A)
        wv_b, cv = cast_weight("aWv", D)
        wog_b, cog = cast_weight("aWog", D)

        wts[0] = A.carve("wt0", 16384, 128, NDT * 512, BF16)
        wts[1] = A.carve("wt1", 20480, 128, NDT * 512, BF16)
        qstage = A.carve("qstage", 24576, 128, 2 * 512, BF16)

        project(wq_b, QK_A, cq, "B", mk_evac(qta_hbm, qstage, scale=SCALE_A, form="B"), xnT, xnT_ready)
        project(wk_b, QK_A, ck, "B", mk_evac(kta_hbm, qstage, form="B"), xnT, xnT_ready)
        project(wv_b, D, cv, "A", mk_evac(va_hbm, qstage, form="A"), xnT, xnT_ready)
        project(wog_b, D, cog, "A", mk_evac(sig_hbm, qstage, mode="sig", form="A"), xnT, xnT_ready)
        A.free("wt0", "wt1", "qstage")
        proj_done_st = st.n
        if "proj_a" in taps:
            for nm, hb, shp in (("qta", qta_hbm, (QK_A, TOK)), ("kta", kta_hbm, (QK_A, TOK)),
                                 ("va", va_hbm, (TOK, D)), ("siga", sig_hbm, (TOK, D))):
                DOUT(f"dbg_{nm}", shp, BF16)
                P.wait("gpsimd", st, proj_done_st)
                for rr in range(0, shp[0], 512):
                    dma(stg, "gpsimd", dbg[f"dbg_{nm}"][rr:rr + 512, :], hb[rr:rr + 512, :])
        return xnT, xnT_ready

    def ph_attn_a_gates_scan(xnT, xnT_ready):
        SS = scan_stage
        gam = A.carve("gam", 33280, 128, D, F32)
        dma(ld, "sync", gam, inp["a_gamma_rep"][:, :])
        idf = A.carve("idf", 35328, 128, 128, F32)
        P.wait("vector", ld)
        last_inc("vector", ve, lambda e: e.tensor_copy(idf, id_sb[:, :]))
        idf_ready = ve.n

        wig_sb = A.carve("wig_sb", 35456, 128, 2 * D, BF16)
        dma(cast, "gpsimd", wig_sb[0:NH_A, 0:D], inp["aWig"][:, :])
        dma(cast, "gpsimd", wig_sb[0:NH_A, D:2 * D], inp["aWfg"][:, :])
        gcast = cast.n
        wigT = A.carve("wigT", 37504, 128, 2 * NDT * 8, BF16)
        P.wait("tensor", cast, gcast)
        for w in range(2):
            for dt in range(NDT):
                last_inc("tensor", pe, lambda e, w=w, dt=dt: e.transpose(
                    bk16(7)[:, 0:8], wig_sb[0:NH_A, w * D + dt * 128: w * D + (dt + 1) * 128], id_sb[0:NH_A, 0:NH_A]))
                P.wait("vector", pe)
                last_inc("vector", ve, lambda e, w=w, dt=dt: e.tensor_copy(wigT[:, (w * NDT + dt) * 8:(w * NDT + dt + 1) * 8], bk16(7)[:, 0:8]))
                P.wait("tensor", ve)
        gi = A.carve("gi", 0, NH_A, 1024, F32)
        gf = A.carve("gf", 1024, NH_A, 1024, F32)
        glf = A.carve("glf", 2048, NH_A, 1024, F32)
        glb = A.carve("glb", 3072, NH_A, 1024, F32)
        ga = A.carve("ga", 4096, NH_A, 1024, F32)
        ggp = A.carve("ggp", 5120, NH_A, 1024, F32)
        gG = A.carve("gG", 6144, NH_A, 1024, F32)
        pingA = A.carve("pingA", 7168, 128, 1024, F32)
        pingB = A.carve("pingB", 37632, 128, 1024, F32)
        gsc = A.carve("gsc", 38656, NH_A, 1024, F32)
        gsc2 = A.carve("gsc2", 39680, NH_A, 1024, F32)
        P.wait("tensor", tr, xnT_ready)
        P.wait("tensor", ld, biasg_ld)
        P.wait("tensor", ve, biasg_ready)
        for w, gdst in ((0, gi), (1, gf)):
            for th in range(2):
                if 6 in bankmark:
                    s_, c_ = bankmark[6]
                    P.wait("tensor", s_, c_)
                for dt in range(NDT):
                    fn = (lambda e, w=w, dt=dt, th=th, s=(dt == 0), p=(dt == NDT - 1):
                          e.matmul(banks[6][0:NH_A, :], wigT[:, (w * NDT + dt) * 8:(w * NDT + dt + 1) * 8],
                                   xnT[:, dt * TOK + th * 512: dt * TOK + (th + 1) * 512], start=s, stop=p))
                    if dt == NDT - 1:
                        last_inc("tensor", pe, fn)
                    else:
                        top(fn)
                P.wait("scalar", pe)
                P.wait("scalar", ve, biasg_ready)
                last_inc("scalar", se, lambda e, w=w, th=th:
                         e.activation(out=gsc[:, th * 512:(th + 1) * 512], in_=banks[6][0:NH_A, :],
                                      func=AF.Tanh, scale=1.0 / CAP_A, bias=misc[0:NH_A, 2 + w:3 + w]))
                bankmark[6] = (se, se.n)
                P.wait("tensor", se)
            P.wait("vector", se)
            last_inc("vector", ve, lambda e, gdst=gdst: e.tensor_scalar(gdst, gsc[0:NH_A, :], CAP_A, None, OP.mult))
        gates_done = ve.n

        P.wait("scalar", ve, gates_done)
        last_inc("scalar", se, lambda e: e.activation(out=gsc[0:NH_A, :], in_=gf, func=AF.Abs))
        last_inc("scalar", se, lambda e: e.activation(out=gsc[0:NH_A, :], in_=gsc[0:NH_A, :], func=AF.Exp, scale=-1.0))
        last_inc("scalar", se, lambda e: e.activation(out=gsc[0:NH_A, :], in_=gsc[0:NH_A, :], func=AF.Ln, bias=1.0))
        P.wait("vector", se)
        vop(lambda e: e.tensor_scalar(gsc2[0:NH_A, :], gf, 0.0, None, OP.min))
        last_inc("vector", ve, lambda e: e.tensor_tensor(glf, gsc2[0:NH_A, :], gsc[0:NH_A, :], OP.subtract))

        def scan_op(src_ap, dst_slot_ap, op):
            a_ = pingA[0:NH_A, :]
            b_ = pingB[0:NH_A, :]
            last_inc("vector", ve, lambda e: e.tensor_copy(a_, src_ap))
            cur, nxt = a_, b_
            for k in range(10):
                s = 1 << k
                P.wait("vector", ve)
                vop(lambda e, c=cur, n=nxt, s=s: e.tensor_copy(n[:, 0:s], c[:, 0:s]))
                last_inc("vector", ve, lambda e, c=cur, n=nxt, s=s: e.tensor_tensor(n[:, s:TOK], c[:, s:TOK], c[:, 0:TOK - s], op))
                cur, nxt = nxt, cur
            last_inc("vector", ve, lambda e, c=cur: e.tensor_copy(dst_slot_ap, c))

        scan_op(glf, glb, OP.add)
        last_inc("vector", ve, lambda e: e.tensor_tensor(ga, gi, glb, OP.subtract))
        scan_op(ga, ggp, OP.max)
        gmath_done = ve.n
        if "gates0" in taps:
            DOUT("dbg_gates0", (5 * NH_A, TOK))
            P.wait("gpsimd", ve, gmath_done)
            for i, ap in enumerate((gi, gf, glf, glb, ga)):
                dma(stg, "gpsimd", dbg["dbg_gates0"][i * NH_A:(i + 1) * NH_A, :], ap)
            tap_barrier()

        if SS < 2:
            return st.n
        COLS = A.carve("COLS", 40704, 128, 704, F32)
        acol = COLS[:, 0:64]; lbcol = COLS[:, 64:128]; Gcol = COLS[:, 128:192]
        expnegm = COLS[:, 192:256]; wcol = COLS[:, 256:320]; Fcol = COLS[:, 320:384]
        aadj = COLS[:, 384:448]; decay_bc = COLS[:, 448:512]; rows_sc = COLS[:, 512:576]
        f1bc = COLS[:, 576:640]; f2bc = COLS[:, 640:704]

        def transpose_gate_cols(src_ap, dst_col_ap, ready_cnt):
            P.wait("tensor", ve, ready_cnt)
            P.wait("tensor", ve, idf_ready)
            bp = src_ap.base_partition()
            for J in range(NCH):
                last_inc("tensor", pe, lambda e, J=J, s=src_ap, bp=bp:
                         e.transpose(banks[7][:, 0:8], s[:, J * 128:(J + 1) * 128], idf[bp:bp + NH_A, bp:bp + NH_A]))
                P.wait("vector", pe)
                last_inc("vector", ve, lambda e, J=J, d=dst_col_ap: e.tensor_copy(d[:, J:64:8], banks[7][:, 0:8]))
                P.wait("tensor", ve)

        transpose_gate_cols(ga, acol, gmath_done)
        transpose_gate_cols(glb, lbcol, gmath_done)

        def extract_row(src_gate_ap, dst_row_ap, hbmt):
            P.wait("vector", ve)
            last_inc("vector", ve, lambda e: e.tensor_copy(misc[0:NH_A, 40:48], src_gate_ap[:, 127:TOK:128]))
            P.wait("sync", ve)
            dma(st, "sync", hbmt[0:NH_A, 0:NH_A], misc[0:NH_A, 40:48])
            P.wait("sync", st)
            dma(ld, "sync", dst_row_ap, hbmt.rearrange("a b -> (a b)").unsqueeze(0))

        gpe_row = misc[0:1, 64:128]
        gpeprev_row = misc[0:1, 128:192]
        extract_row(ggp, gpe_row, rA_hbm)
        P.wait("vector", ld)
        vser(lambda e: e.tensor_copy(gpeprev_row[:, 1:64], gpe_row[:, 0:63]))
        vser(lambda e: e.tensor_copy(gpeprev_row[:, 0:64:8], gpe_row[:, 0:64:8]))
        rows_ready = ve.n

        def bcast64(row_ap, out_ap):
            P.wait("tensor", ve)
            P.wait("tensor", ld)
            if 7 in bankmark:
                _s7, _c7 = bankmark[7]
                P.wait("tensor", _s7, _c7)
            last_inc("tensor", pe, lambda e, r=row_ap: e.matmul(banks[7][:, 0:64], onesf[0:1, 0:128], r, start=True, stop=True))
            P.wait("vector", pe)
            last_inc("vector", ve, lambda e, o=out_ap: e.tensor_copy(o, banks[7][:, 0:64]))
            bankmark[7] = (ve, ve.n)
            P.wait("tensor", ve)

        gpe_bc = COLS[:, 512:576]
        gpeprev_bc = COLS[:, 576:640]
        bcast64(gpe_row, gpe_bc)
        bcast64(gpeprev_row, gpeprev_bc)
        P.wait("vector", ve)
        last_inc("vector", ve, lambda e: e.tensor_tensor(pingA[:, 0:64], acol, gpe_bc, OP.subtract))
        last_inc("vector", ve, lambda e: e.tensor_tensor(pingA[:, 64:128], gpeprev_bc, gpe_bc, OP.subtract))
        P.wait("scalar", ve)
        last_inc("scalar", se, lambda e: e.activation(out=wcol, in_=pingA[:, 0:64], func=AF.Exp))
        last_inc("scalar", se, lambda e: e.activation(out=decay_bc, in_=pingA[:, 64:128], func=AF.Exp))
        loc_cols_ready = se.n

        cstate = A.carve("cstate", 16384, 128, NH_A * 257, F32)
        cstateb = A.carve("cstateb", 18448, 128, NH_A * 257, BF16)
        vroll = A.carve("vroll", 19488, 128, 2 * D, BF16)
        knroll = A.carve("knroll", 21536, 128, 2 * 1024, BF16)
        ktroll = A.carve("ktroll", 22560, 128, 2 * 1024, BF16)
        qtroll = A.carve("qtroll", 23584, 128, 2 * 1024, BF16)
        sigroll = A.carve("sigroll", 24608, 128, 2 * D, BF16)
        hraw = A.carve("hraw", 26656, 128, NH_A * 257, F32)
        houtst = A.carve("houtst", 28720, 128, 2 * D, BF16)
        e1buf = A.carve("e1buf", 30768, 128, 2 * 128, F32)
        wbuf = A.carve("wbuf", 31024, 128, 2 * 128, BF16)
        wvbuf = A.carve("wvbuf", 31152, 128, 2 * 260, BF16)
        udump = A.carve("udump", 31424, 128, 256, F32)
        ustage = A.carve("ustage", 31680, 128, 256, F32)

        last_inc("vector", ve, lambda e: e.memset(cstate[:, :], 0.0))
        czero = ve.n

        proj_all_done = st.n

        def state_pass(main):
            ldmark = {}
            for J in range(NCH):
                b = J % 2
                if J >= 2 and (J - 2) in ldmark:
                    s, c = ldmark[J - 2]
                    P.wait("sync", s, c)
                P.wait("sync", st, proj_all_done)
                dma(ld, "sync", vroll[:, b * D:(b + 1) * D], va_hbm[J * 128:(J + 1) * 128, :])
                dmaT(knroll[:, b * 1024:(b + 1) * 1024], kta_hbm[:, J * 128:(J + 1) * 128])
                if main:
                    for h in range(NH_A):
                        dma(ld, "sync", ktroll[:, b * 1024 + h * 128:b * 1024 + (h + 1) * 128],
                            kta_hbm[h * 128:(h + 1) * 128, J * 128:(J + 1) * 128])
                        dma(ld, "sync", qtroll[:, b * 1024 + h * 128:b * 1024 + (h + 1) * 128],
                            qta_hbm[h * 128:(h + 1) * 128, J * 128:(J + 1) * 128])
                    dma(ld, "sync", sigroll[:, b * D:(b + 1) * D], sig_hbm[J * 128:(J + 1) * 128, :])
                ld_done = ld.n
                tr_done = tr.n
                for h in range(NH_A):
                    col = h * 8 + J
                    wvb = wvbuf[:, (h % 2) * 260:(h % 2) * 260 + 257]
                    P.wait("vector", ld, ld_done)
                    P.wait("vector", tr, tr_done)
                    P.wait("vector", pe)
                    vop(lambda e, w=wvb, b=b, h=h, col=col:
                        e.tensor_scalar(w[:, 0:256], vroll[:, b * D + h * 256:b * D + (h + 1) * 256], wcol[:, col:col + 1], None, OP.mult))
                    last_inc("vector", ve, lambda e, w=wvb, col=col: e.tensor_copy(w[:, 256:257], wcol[:, col:col + 1]))
                    if main:
                        sbank = banks[4 + (h % 2)]
                        P.wait("tensor", ld, ld_done)
                        if (4 + h % 2) in bankmark:
                            s, c = bankmark[4 + h % 2]
                            P.wait("tensor", s, c)
                        pe.bump(1)
                        P.emit("tensor", lambda e, sb=sbank, b=b, h=h, hh=pe.h:
                               e.matmul(sb[:, 0:128], ktroll[:, b * 1024 + h * 128:b * 1024 + (h + 1) * 128],
                                        qtroll[:, b * 1024 + h * 128:b * 1024 + (h + 1) * 128], start=True, stop=True).then_inc(hh, 1))
                        e1 = e1buf[:, (h % 2) * 128:(h % 2) * 128 + 128]
                        P.wait("scalar", ve)
                        P.wait("scalar", se, loc_cols_ready)
                        last_inc("scalar", se, lambda e, o=e1, col=col:
                                 e.activation(out=o, in_=mask_sb[:, :], func=AF.Exp, bias=aadj[:, col:col + 1]))
                        wb_ = wbuf[:, (h % 2) * 128:(h % 2) * 128 + 128]
                        P.wait("vector", pe)
                        P.wait("vector", se)
                        last_inc("vector", ve, lambda e, o=wb_, sb=sbank, e1=e1:
                                 e.tensor_tensor(o, sb[:, 0:128], e1, OP.mult))
                        bankmark[4 + h % 2] = (ve, ve.n)
                        hbank = banks[0 + (h % 2)]
                        if (h % 2) in bankmark:
                            s, c = bankmark[h % 2]
                            P.wait("tensor", s, c)
                        P.wait("tensor", ve)
                        top(lambda e, hb=hbank, w=wb_, b=b, h=h:
                            e.matmul(hb[:, 0:256], w, vroll[:, b * D + h * 256:b * D + (h + 1) * 256], start=True, stop=False))
                        top(lambda e, hb=hbank, w=wb_: e.matmul(hb[:, 256:257], w, onescol_b[:, 0:1], start=False, stop=False))
                        pe.bump(1)
                        P.emit("tensor", lambda e, hb=hbank, b=b, h=h, hh=pe.h:
                               e.matmul(hb[:, 0:257], qtroll[:, b * 1024 + h * 128:b * 1024 + (h + 1) * 128],
                                        cstateb[:, h * 257:(h + 1) * 257], start=False, stop=True).then_inc(hh, 1))
                        mm3_done = pe.n
                        P.wait("vector", pe, mm3_done)
                        last_inc("vector", ve, lambda e, hb=hbank, h=h: e.tensor_copy(hraw[:, h * 257:(h + 1) * 257], hb[:, 0:257]))
                        bankmark[h % 2] = (ve, ve.n)
                    stbank = banks[2 + (h % 2)]
                    if (2 + h % 2) in bankmark:
                        s, c = bankmark[2 + h % 2]
                        P.wait("tensor", s, c)
                    P.wait("tensor", ve)
                    P.wait("tensor", tr, tr_done)
                    pe.bump(1)
                    P.emit("tensor", lambda e, sb=stbank, b=b, h=h, w=wvb, hh=pe.h:
                           e.matmul(sb[:, 0:257], knroll[:, b * 1024 + h * 128:b * 1024 + (h + 1) * 128], w, start=True, stop=True).then_inc(hh, 1))
                    st_done = pe.n
                    P.wait("vector", pe, st_done)
                    vop(lambda e, sb=stbank, h=h, col=col:
                        e.scalar_tensor_tensor(cstate[:, h * 257:(h + 1) * 257], cstate[:, h * 257:(h + 1) * 257],
                                               decay_bc[:, col:col + 1], sb[:, 0:257], OP.mult, OP.add))
                    last_inc("vector", ve, lambda e, h=h: e.tensor_copy(cstateb[:, h * 257:(h + 1) * 257], cstate[:, h * 257:(h + 1) * 257]))
                    bankmark[2 + h % 2] = (ve, ve.n)
                if main:
                    emit_chunk_epilogue(J)
                ldmark[J] = (ve, ve.n)
            return ldmark

        def emit_chunk_epilogue(J):
            P.wait("vector", ve)
            vser(lambda e, J=J: e.tensor_tensor(stat[:, 200:208], hraw[:, 256:NH_A * 257:257], Fcol[:, J:64:8], OP.mult))
            sser(lambda e: e.activation(out=stat[:, 208:216], in_=stat[:, 200:208], func=AF.Abs))
            P.wait("vector", se)
            vser(lambda e, J=J: e.tensor_tensor(stat[:, 216:224], stat[:, 208:216], expnegm[:, J:64:8], OP.max))
            vser(lambda e: e.tensor_scalar(stat[:, 216:224], stat[:, 216:224], EPS_A, None, OP.add))
            vser(lambda e: e.reciprocal(stat[:, 224:232], stat[:, 216:224]))
            vser(lambda e, J=J: e.tensor_tensor(stat[:, 232:240], stat[:, 224:232], Fcol[:, J:64:8], OP.mult))
            for h in range(NH_A):
                vop(lambda e, h=h: e.reduce_sum(stat[:, 240 + h:241 + h], hraw[:, h * 257:h * 257 + 256], axis=AX.X))
                last_inc("scalar", se, lambda e, h=h: e.activation(out=udump, in_=hraw[:, h * 257:h * 257 + 256], func=AF.Square, accum_out=stat[:, 248 + h:249 + h]))
            last_inc("vector", ve, lambda e: e.engine_nop())
            P.wait("vector", se)
            P.wait("vector", ve)
            vser(lambda e: e.tensor_scalar(stat[:, 256:264], stat[:, 240:248], 1.0 / 256.0, None, OP.mult))
            vser(lambda e: e.tensor_scalar(stat[:, 264:272], stat[:, 248:256], 1.0 / 256.0, None, OP.mult))
            vser(lambda e: e.tensor_tensor(stat[:, 272:280], stat[:, 256:264], stat[:, 256:264], OP.mult))
            vser(lambda e: e.tensor_tensor(stat[:, 280:288], stat[:, 264:272], stat[:, 272:280], OP.subtract))
            vser(lambda e: e.tensor_tensor(stat[:, 288:296], stat[:, 232:240], stat[:, 232:240], OP.mult))
            vser(lambda e: e.tensor_tensor(stat[:, 296:304], stat[:, 288:296], stat[:, 280:288], OP.mult))
            P.wait("scalar", ve)
            _bea = constcol(EPS_A)
            P.wait("scalar", ve)
            last_inc("scalar", se, lambda e, b=_bea: e.activation(out=stat[:, 304:312], in_=stat[:, 296:304], func=AF.Sqrt, bias=b))
            P.wait("vector", se)
            vser(lambda e: e.reciprocal(stat[:, 312:320], stat[:, 304:312]))

            vser(lambda e: e.tensor_tensor(stat[:, 320:328], stat[:, 312:320], stat[:, 232:240], OP.mult))
            P.wait("vector", ve)
            P.wait("vector", st)
            b = J % 2
            for h in range(NH_A):
                vop(lambda e, h=h: e.tensor_scalar(ustage, hraw[:, h * 257:h * 257 + 256],
                                                   stat[:, 256 + h:257 + h], stat[:, 320 + h:321 + h], OP.subtract, OP.mult))
                vop(lambda e, h=h: e.tensor_tensor(ustage, ustage, gam[:, h * 256:(h + 1) * 256], OP.mult))
                last_inc("vector", ve, lambda e, h=h, b=b:
                         e.tensor_tensor(houtst[:, b * D + h * 256:b * D + (h + 1) * 256], ustage,
                                         sigroll[:, b * D + h * 256:b * D + (h + 1) * 256], OP.mult))
            P.wait("sync", ve)
            dma(st, "sync", hout_hbm[J * 128:(J + 1) * 128, :], houtst[:, b * D:(b + 1) * D])

        if SS < 3:
            return st.n
        P.wait("vector", ve, czero)
        P.wait("vector", se, loc_cols_ready)
        state_pass(main=False)
        if SS < 4:
            return st.n
        P.wait("sync", ve)
        dma(st, "sync", as_in[:, :], cstate[:, :])
        vser(lambda e: e.tensor_copy(misc[0:NH_A, 48:49], glb[:, 1023:1024]))
        vser(lambda e: e.tensor_tensor(misc[0:NH_A, 49:50], glb[:, 1023:1024], ggp[:, 1023:1024], OP.add))
        P.wait("vector", ve)
        P.wait("sync", ve)
        dma(st, "sync", rLam_hbm[0:NH_A, 0:1], misc[0:NH_A, 48:49])
        dma(st, "sync", rMu_hbm[0:NH_A, 0:1], misc[0:NH_A, 49:50])
        P.wait("sync", st)
        dma(st, "sync", am_in[0:1, 0:8], rMu_hbm.rearrange("a b -> (a b)").unsqueeze(0))
        dma(st, "sync", am_in[0:1, 8:16], rLam_hbm.rearrange("a b -> (a b)").unsqueeze(0))
        P.wait("gpsimd", st)
        cc.bump(1)
        P.emit("gpsimd", lambda e, h=cc.h: e.collective_compute(
            "AllGather", mybir.AluOpType.bypass, replica_groups=[list(range(NCORES))],
            ins=[as_in.opt()], outs=[as_g.opt()]).then_inc(h, 1))
        cc.bump(1)
        P.emit("gpsimd", lambda e, h=cc.h: e.collective_compute(
            "AllGather", mybir.AluOpType.bypass, replica_groups=[list(range(NCORES))],
            ins=[am_in.opt()], outs=[am_g.opt()]).then_inc(h, 1))
        ag_done = cc.n

        amg = misc[0:NCORES, 200:216]
        rmask = misc[0:NCORES, 216:218]
        P.wait("sync", cc, ag_done)
        dma(ld, "sync", amg, am_g[:, :])
        dma(ld, "sync", rmask, inp["rankmask"][:, :])
        P.wait("vector", ld)
        vser(lambda e: e.tensor_scalar(misc[0:NCORES, 220:228], amg[:, 0:8], rmask[:, 0:1], None, OP.add))
        vser(lambda e: e.tensor_scalar(misc[0:NCORES, 228:236], amg[:, 8:16], rmask[:, 1:2], None, OP.mult))
        P.wait("tensor", ve)
        for s0, d0 in ((220, 236), (228, 244)):
            last_inc("tensor", pe, lambda e, s0=s0: e.transpose(banks[7][0:NCORES, 0:8], misc[0:NCORES, s0:s0 + 8], idf[0:NCORES, 0:NCORES]))
            P.wait("vector", pe)
            last_inc("vector", ve, lambda e, d0=d0: e.tensor_copy(misc[0:NH_A, d0:d0 + 8], banks[7][0:NH_A, 0:8]))
            P.wait("tensor", ve)
        mu_t = misc[0:NH_A, 236:244]
        lam_t = misc[0:NH_A, 244:252]
        last_inc("vector", ve, lambda e: e.memset(misc[0:NH_A, 252:253], 0.0))
        for r in range(NCORES):
            vser(lambda e, r=r: e.tensor_tensor(misc[0:NH_A, 253:254], lam_t[:, r:r + 1], misc[0:NH_A, 252:253], OP.add))
            vser(lambda e, r=r: e.tensor_tensor(misc[0:NH_A, 254:255], mu_t[:, r:r + 1], misc[0:NH_A, 253:254], OP.max))
            vser(lambda e, r=r: e.tensor_tensor(misc[0:NH_A, 256 + r:257 + r], mu_t[:, r:r + 1], misc[0:NH_A, 254:255], OP.subtract))
            vser(lambda e, r=r: e.tensor_tensor(misc[0:NH_A, 264 + r:265 + r], misc[0:NH_A, 253:254], misc[0:NH_A, 254:255], OP.subtract))
            vser(lambda e: e.tensor_copy(misc[0:NH_A, 252:253], misc[0:NH_A, 254:255]))
        P.wait("scalar", ve)
        last_inc("scalar", se, lambda e: e.activation(out=misc[0:NH_A, 272:280], in_=misc[0:NH_A, 256:264], func=AF.Exp))
        last_inc("scalar", se, lambda e: e.activation(out=misc[0:NH_A, 280:288], in_=misc[0:NH_A, 264:272], func=AF.Exp))
        P.wait("sync", se)
        dma(st, "sync", rB_hbm[0:NH_A, 0:NH_A], misc[0:NH_A, 272:280])
        dma(st, "sync", rC_hbm[0:NH_A, 0:NH_A], misc[0:NH_A, 280:288])
        P.wait("sync", st)
        f1row = misc[0:1, 300:364]
        f2row = misc[0:1, 364:428]
        dma(ld, "sync", f1row, rB_hbm.rearrange("a b -> (a b)").unsqueeze(0))
        dma(ld, "sync", f2row, rC_hbm.rearrange("a b -> (a b)").unsqueeze(0))
        P.wait("tensor", ld)
        bcast64(f1row, f1bc)
        bcast64(f2row, f2bc)
        last_inc("vector", ve, lambda e: e.memset(cstate[:, :], 0.0))
        cshare = A.carve("cshare", 41408, 128, NH_A * 257, F32)
        for r in range(NCORES):
            P.wait("sync", ve)
            dma(ld, "sync", cshare[:, :], as_g[r * 128:(r + 1) * 128, :])
            P.wait("vector", ld)
            for h in range(NH_A):
                col = h * 8 + r
                vop(lambda e, h=h, col=col: e.tensor_scalar(cstate[:, h * 257:(h + 1) * 257], cstate[:, h * 257:(h + 1) * 257], f2bc[:, col:col + 1], None, OP.mult))
                fn = lambda e, h=h, col=col: e.scalar_tensor_tensor(cstate[:, h * 257:(h + 1) * 257], cshare[:, h * 257:(h + 1) * 257], f1bc[:, col:col + 1], cstate[:, h * 257:(h + 1) * 257], OP.mult, OP.add)
                if h == NH_A - 1:
                    last_inc("vector", ve, fn)
                else:
                    vop(fn)
        for h in range(NH_A):
            last_inc("vector", ve, lambda e, h=h: e.tensor_copy(cstateb[:, h * 257:(h + 1) * 257], cstate[:, h * 257:(h + 1) * 257]))
        comb_done = ve.n
        if "comb" in taps:
            DOUT("dbg_minC", (128, NH_A * 257 + 16))
            P.wait("gpsimd", ve, comb_done)
            dma(stg, "gpsimd", dbg["dbg_minC"][:, 0:NH_A * 257], cstate[:, :])
            dma_slow(stg, "gpsimd", dbg["dbg_minC"][0:NH_A, NH_A * 257:NH_A * 257 + 1], misc[0:NH_A, 252:253])
            tap_barrier()

        if SS < 5:
            return st.n
        P.wait("vector", ve, comb_done)
        last_inc("vector", ve, lambda e: e.tensor_scalar(gG, ggp, misc[0:NH_A, 252:253], None, OP.max))
        gG_done = ve.n
        transpose_gate_cols(gG, Gcol, gG_done)
        P.wait("vector", ve)
        last_inc("vector", ve, lambda e: e.tensor_tensor(pingA[:, 0:64], lbcol, Gcol, OP.add))
        P.wait("scalar", ve)
        last_inc("scalar", se, lambda e: e.activation(out=expnegm, in_=pingA[:, 0:64], func=AF.Exp, scale=-1.0))
        gend_row = misc[0:1, 428:492]
        extract_row(gG, gend_row, rA_hbm)
        minrow = misc[0:1, 492:500]
        P.wait("sync", ve)
        dma(st, "sync", rM_hbm[0:NH_A, 0:1], misc[0:NH_A, 252:253])
        P.wait("sync", st)
        dma(ld, "sync", minrow, rM_hbm.rearrange("a b -> (a b)").unsqueeze(0))
        gprev_row = misc[0:1, 500:564]
        P.wait("vector", ld)
        vser(lambda e: e.tensor_copy(gprev_row[:, 1:64], gend_row[:, 0:63]))
        vser(lambda e: e.tensor_copy(gprev_row[:, 0:64:8], minrow[:, 0:8]))
        P.wait("tensor", ve)
        gend_bc = COLS[:, 512:576]
        gprev_bc = COLS[:, 576:640]
        bcast64(gend_row, gend_bc)
        bcast64(gprev_row, gprev_bc)
        P.wait("vector", ve)
        vop(lambda e: e.tensor_tensor(aadj, acol, gprev_bc, OP.subtract))
        vop(lambda e: e.tensor_tensor(pingA[:, 0:64], acol, gend_bc, OP.subtract))
        vop(lambda e: e.tensor_tensor(pingA[:, 64:128], gprev_bc, Gcol, OP.subtract))
        last_inc("vector", ve, lambda e: e.tensor_tensor(pingA[:, 128:192], gprev_bc, gend_bc, OP.subtract))
        P.wait("scalar", ve)
        last_inc("scalar", se, lambda e: e.activation(out=wcol, in_=pingA[:, 0:64], func=AF.Exp))
        last_inc("scalar", se, lambda e: e.activation(out=Fcol, in_=pingA[:, 64:128], func=AF.Exp))
        last_inc("scalar", se, lambda e: e.activation(out=decay_bc, in_=pingA[:, 128:192], func=AF.Exp))
        glob_cols_ready = se.n

        if SS < 6:
            return st.n
        P.wait("vector", se, glob_cols_ready)
        P.wait("tensor", se, glob_cols_ready)
        state_pass(main=True)
        scan_done_st = st.n

        if "gates" in taps:
            DOUT("dbg_gates", (6 * NH_A, TOK))
            P.wait("gpsimd", ve)
            for i, ap in enumerate((gi, gf, glf, glb, ga, gG)):
                dma(stg, "gpsimd", dbg["dbg_gates"][i * NH_A:(i + 1) * NH_A, :], ap)
            tap_barrier()
        if "scan" in taps:
            DOUT("dbg_hout", (TOK, D), BF16)
            P.wait("gpsimd", st, scan_done_st)
            dma(stg, "gpsimd", dbg["dbg_hout"][:, :], hout_hbm[:, :])
            tap_barrier()
        A.free("gam", "idf", "wig_sb", "wigT", "gi", "gf", "glf", "glb", "ga", "ggp",
               "gG", "pingA", "pingB", "gsc", "gsc2", "COLS", "cstate",
               "cstateb", "vroll", "knroll", "ktroll", "qtroll", "sigroll", "hraw", "houtst",
               "e1buf", "wbuf", "wvbuf", "udump", "ustage", "cshare")
        return scan_done_st

    def ph_attn_a_out(scan_done_st):
        phase_barrier()
        wout_b, cw = cast_weight("aWout", D)
        houtT = A.carve("houtT", 0, 128, NDT * TOK, BF16)
        P.wait("sync", st, scan_done_st)
        for vt in range(NDT):
            dmaT(houtT[:, vt * TOK:(vt + 1) * TOK], hout_hbm[:, vt * 128:(vt + 1) * 128])
        hT_ready = tr.n
        wts[0] = A.carve("wto0", 16384, 128, NDT * 512, BF16)
        wts[1] = A.carve("wto1", 20480, 128, NDT * 512, BF16)
        x0roll = A.carve("x0roll", 24576, 128, 2 * D, F32)
        x1st = A.carve("x1st", 28672, 128, 2 * D, F32)
        x0mark = {}

        wts2 = [A.carve(f"wto{i+2}", 32768 + i * 4096, 128, NDT * 512, BF16) for i in range(2)]
        allw = [wts[0], wts[1], wts2[0], wts2[1]]
        P.wait("sync", cast, cw)
        for oci in range(4):
            for dt in range(NDT):
                dmaT(allw[oci][:, dt * 512:(dt + 1) * 512], wcast_hbm["aWout"][oci * 512:(oci + 1) * 512, dt * 128:(dt + 1) * 128])
        w_ready = tr.n
        P.wait("tensor", tr, w_ready)
        P.wait("tensor", tr, hT_ready)
        for tt in range(NT):
            b = tt % 2
            if tt >= 2 and (tt - 2) in x0mark:
                s, c = x0mark[tt - 2]
                P.wait("sync", s, c)
            dma(ld, "sync", x0roll[:, b * D:(b + 1) * D], inp["x0"][tt * 128:(tt + 1) * 128, :])
            ld_x0 = ld.n
            for oci in range(4):
                bidx = oci % 4
                bank = banks[bidx]
                if bidx in bankmark:
                    s, c = bankmark[bidx]
                    P.wait("tensor", s, c)
                for dt in range(NDT):
                    fn = (lambda e, b2=bank, l=houtT[:, dt * TOK + tt * 128: dt * TOK + (tt + 1) * 128],
                          r=allw[oci][:, dt * 512:(dt + 1) * 512], s=(dt == 0), p=(dt == NDT - 1):
                          e.matmul(b2[:, :], l, r, start=s, stop=p))
                    if dt == NDT - 1:
                        last_inc("tensor", pe, fn)
                    else:
                        top(fn)
                P.wait("vector", pe)
                P.wait("vector", ld, ld_x0)
                fn = lambda e, bank=bank, tt=tt, b=b, oci=oci: e.scalar_tensor_tensor(
                    x1st[:, b * D + oci * 512:b * D + (oci + 1) * 512], bank[:, :], 1.0,
                    x0roll[:, b * D + oci * 512:b * D + (oci + 1) * 512], OP.mult, OP.add)
                last_inc("vector", ve, fn)
                bankmark[bidx] = (ve, ve.n)
            P.wait("sync", ve)
            dma(st, "sync", xspA[tt * 128:(tt + 1) * 128, :], x1st[:, b * D:(b + 1) * D])
            x0mark[tt] = (st, st.n)
        x1_done = st.n
        A.free("houtT", "wto0", "wto1", "wto2", "wto3", "x0roll", "x1st")
        if "x1" in taps:
            DOUT("dbg_x1", (TOK, D))
            P.wait("gpsimd", st, x1_done)
            dma(stg, "gpsimd", dbg["dbg_x1"][:, :], xspA[:, :])
            tap_barrier()
        return x1_done

    def ph_ffn_quantprep():
        phase_barrier()
        wroll = A.carve("fq_wroll", 0, 128, 2 * 1024, F32)
        qstg = A.carve("fq_qstg", 2048, 128, 2 * 1024, BF16)
        ist = A.carve("fq_ist", 3072, 128, 1024, I32)
        fst = A.carve("fq_fst", 4096, 128, 1024, F32)
        tcol = A.carve("fq_tcol", 5120, 128, 64, F32)
        slotm = {}
        gct = [0]
        for wi, wsrc in enumerate((inp["cfc_sl"], inp["cpr_sl"])):
            for t in range(NDT):
                g = gct[0]; gct[0] += 1
                b = (g % 2) * 1024
                if g - 2 in slotm:
                    s_, c_ = slotm[g - 2]
                    P.wait("gpsimd", s_, c_)
                dma(cast, "gpsimd", wroll[:, b:b + 1024], wsrc[t * 128:(t + 1) * 128, :])
                P.wait("vector", cast)
                vser(lambda e, wi=wi, t=t, b=b: e.reduce_sum(tcol[:, wi * 16 + t:wi * 16 + t + 1], wroll[:, b:b + 1024], axis=AX.X, apply_absolute_value=True))
                slotm[g] = (ve, ve.n)
        vser(lambda e: e.reduce_sum(misc[:, 330:331], tcol[:, 0:16], axis=AX.X))
        vser(lambda e: e.reduce_sum(misc[:, 331:332], tcol[:, 16:32], axis=AX.X))
        P.wait("sync", ve)
        dma(st, "sync", ar_in[:, :], misc[:, 330:332])
        P.wait("gpsimd", st)
        cc.bump(1)
        P.emit("gpsimd", lambda e, h=cc.h: e.collective_compute(
            "AllReduce", mybir.AluOpType.add, replica_groups=[list(range(NCORES))],
            ins=[ar_in.opt()], outs=[ar_out.opt()]).then_inc(h, 1))
        P.wait("sync", cc)
        dma(ld, "sync", misc[:, 332:334], ar_out[:, :])
        P.wait("tensor", ld)
        P.wait("tensor", ve)
        if 6 in bankmark:
            _s6, _c6 = bankmark[6]
            P.wait("tensor", _s6, _c6)
        last_inc("tensor", pe, lambda e: e.matmul(banks[6][0:1, 0:2], onesf[:, 0:1], misc[:, 332:334], start=True, stop=True))
        P.wait("vector", pe)
        vser(lambda e: e.tensor_scalar(misc[0:1, 334:336], banks[6][0:1, 0:2], 1.0 / (D * H_FFN), EPS_FFN, OP.mult, OP.max))
        P.wait("tensor", ve)
        last_inc("tensor", pe, lambda e: e.matmul(banks[6][:, 4:6], onesf[0:1, 0:128], misc[0:1, 334:336], start=True, stop=True))
        P.wait("vector", pe)
        vser(lambda e: e.tensor_copy(misc[:, 336:338], banks[6][:, 4:6]))
        bankmark[6] = (ve, ve.n)
        vser(lambda e: e.reciprocal(misc[:, 338:340], misc[:, 336:338]))
        vser(lambda e: e.tensor_scalar(misc[:, 340:341], misc[:, 337:338], 1.0 / 255.0, None, OP.mult))
        vser(lambda e: e.tensor_tensor(misc[:, 341:342], misc[:, 336:337], misc[:, 336:337], OP.mult))
        vser(lambda e: e.tensor_tensor(misc[:, 340:341], misc[:, 340:341], misc[:, 341:342], OP.mult))
        qsc_ready = ve.n

        def quant_stream(wsrc, dst_hbm, inv_col):
            qmark = {}
            for t in range(NDT):
                g = gct[0]; gct[0] += 1
                b = (g % 2) * 1024
                if g - 2 in slotm:
                    s_, c_ = slotm[g - 2]
                    P.wait("gpsimd", s_, c_)
                dma(cast, "gpsimd", wroll[:, b:b + 1024], wsrc[t * 128:(t + 1) * 128, :])
                P.wait("vector", cast)
                P.wait("vector", ve, qsc_ready)
                vser(lambda e, b=b, iv=inv_col: e.tensor_scalar(fst[:, :], wroll[:, b:b + 1024], iv, None, OP.mult))
                slotm[g] = (ve, ve.n)
                vser(lambda e: e.tensor_copy(ist[:, :], fst[:, :]))
                vser(lambda e: e.tensor_copy(fst[:, :], ist[:, :]))
                if t >= 2 and (t - 2) in qmark:
                    s, c = qmark[t - 2]
                    P.wait("vector", s, c)
                vser(lambda e, t=t, b=b: e.tensor_scalar(qstg[:, b:b + 1024], fst[:, :], -1.0, 1.0, OP.max, OP.min))
                P.wait("sync", ve)
                dma(st, "sync", dst_hbm[t * 128:(t + 1) * 128, :], qstg[:, b:b + 1024])
                qmark[t] = (st, st.n)
        quant_stream(inp["cfc_sl"], w1q_in, misc[:, 338:339])
        quant_stream(inp["cpr_sl"], w2n_hbm, misc[:, 339:340])
        P.wait("sync", st)
        tmark = {}
        A.free("fq_wroll", "fq_ist", "fq_fst")
        tstg = A.carve("fq_tstg", 0, 128, 2 * D, BF16)
        for c in range(8):
            b = (c % 2) * D
            if c >= 2 and (c - 2) in tmark:
                s, cc_ = tmark[c - 2]
                P.wait("sync", s, cc_)
            dmaT(tstg[:, b:b + D], w2n_hbm[:, c * 128:(c + 1) * 128])
            P.wait("sync", tr)
            dma(st, "sync", w2qT_in[c * 128:(c + 1) * 128, :], tstg[:, b:b + D])
            tmark[c] = (st, st.n)
        P.wait("gpsimd", st)
        for src_t, dst_t in ((w1q_in, w1q_g), (w2qT_in, w2qT_g)):
            cc.bump(1)
            P.emit("gpsimd", lambda e, h=cc.h, s=src_t, d=dst_t: e.collective_compute(
                "AllGather", mybir.AluOpType.bypass, replica_groups=[list(range(NCORES))],
                ins=[s.opt()], outs=[d.opt()]).then_inc(h, 1))
        if "ffnq" in taps:
            DOUT("dbg_ffnq_w1in", (D, 64), BF16)
            P.wait("gpsimd", cc)
            dma(stg, "gpsimd", dbg["dbg_ffnq_w1in"][:, :], w1q_in[:, 0:64])
            DOUT("dbg_ffnq_sc", (128, 16))
            DOUT("dbg_ffnq_w1", (NCORES * D, 64), BF16)
            DOUT("dbg_ffnq_w2", (512, D), BF16)
            P.wait("gpsimd", cc)
            dma(stg, "gpsimd", dbg["dbg_ffnq_sc"][:, 0:12], misc[:, 330:342])
            for rk in range(NCORES):
                for half in range(4):
                    dma(stg, "gpsimd", dbg["dbg_ffnq_w1"][rk * D + half * 512:rk * D + (half + 1) * 512, :], w1q_g[rk * D + half * 512:rk * D + (half + 1) * 512, 0:64])
            dma(stg, "gpsimd", dbg["dbg_ffnq_w2"][:, :], w2qT_g[0:512, :])
            tap_barrier()
        A.free("fq_qstg", "fq_tcol", "fq_tstg")
        return cc.n

    def ph_ffn(tag, src_dram, dst_dram, ffn_cc_done, final=False):
        xnT = A.carve(f"xnT_{tag}", 32768, 128, NDT * TOK, BF16)
        ph_layernorm("ln_" + tag, src_dram, xnT)
        xnT_ready = tr.n
        zw = A.carve(f"z_{tag}", 0, 128, 8 * H_FFN, BF16)
        wbuf = A.carve(f"wb_{tag}", 40960, 128, 2 * NDT * 256, BF16)
        phase_barrier()
        P.wait("sync", cc, ffn_cc_done)
        wmark = {}
        ci = 0
        gi_ = 0
        for hb2 in range(32):
            hc0 = hb2 * 256
            b = (ci % 2) * (NDT * 256)
            if ci >= 2 and (ci - 2) in wmark:
                s, c = wmark[ci - 2]
                P.wait("sync", s, c)
            r = hc0 // 1024
            cw = hc0 % 1024
            for dt in range(NDT):
                dma(ld, "sync", wbuf[:, b + dt * 256:b + (dt + 1) * 256],
                    w1q_g[r * D + dt * 128:r * D + dt * 128 + 128, cw:cw + 256])
            ld_done = ld.n
            for tt in range(NT):
                bidx = (gi_ % 4) if not SERIAL_PSUM else 0
                bank = banks[bidx]
                if bidx in bankmark:
                    s, c = bankmark[bidx]
                    P.wait("tensor", s, c)
                P.wait("tensor", ld, ld_done)
                if gi_ == 0:
                    P.wait("tensor", tr, xnT_ready)
                for dt in range(NDT):
                    fn = (lambda e, bk=bank, l=xnT[:, dt * TOK + tt * 128: dt * TOK + (tt + 1) * 128],
                          rr=wbuf[:, b + dt * 256:b + (dt + 1) * 256], s=(dt == 0), p=(dt == NDT - 1):
                          e.matmul(bk[:, 0:256], l, rr, start=s, stop=p))
                    if dt == NDT - 1:
                        last_inc("tensor", pe, fn)
                    else:
                        top(fn)
                P.wait("scalar", pe)
                last_inc("scalar", se, lambda e, bk=bank, tt=tt, hc0=hc0:
                         e.activation(out=zw[:, tt * H_FFN + hc0: tt * H_FFN + hc0 + 256], in_=bk[:, 0:256], func=AF.Relu))
                bankmark[bidx] = (se, se.n)
                gi_ += 1
            wmark[ci] = (pe, pe.n)
            ci += 1
        P.wait("vector", se)
        for tt in range(NT):
            vser(lambda e, tt=tt: e.reduce_max(stat[:, 400 + tt:401 + tt], zw[:, tt * H_FFN:(tt + 1) * H_FFN], axis=AX.X))
        if ("ffr_" + tag) in taps:
            DOUT(f"dbg_ffr_{tag}", (128, 8 * H_FFN), BF16)
            P.wait("gpsimd", se)
            P.wait("gpsimd", ve)
            dma(stg, "gpsimd", dbg[f"dbg_ffr_{tag}"][:, :], zw[:, :])
            tap_barrier()
        vser(lambda e: e.tensor_scalar(stat[:, 408:416], stat[:, 400:408], 0.0, None, OP.max))
        vser(lambda e: e.tensor_tensor(stat[:, 416:424], stat[:, 408:416], stat[:, 408:416], OP.mult))
        vser(lambda e: e.tensor_scalar(stat[:, 424:432], stat[:, 416:424], EPS_FFN, None, OP.max))
        vser(lambda e: e.reciprocal(stat[:, 432:440], stat[:, 424:432]))
        vser(lambda e: e.tensor_scalar(stat[:, 440:448], stat[:, 432:440], 255.0, None, OP.mult))
        vser(lambda e: e.tensor_scalar(stat[:, 448:456], stat[:, 424:432], misc[:, 340:341], None, OP.mult))
        quant_cols = ve.n
        ystg = A.carve(f"ys_{tag}", 45056, 128, 1024, F32)
        istg = A.carve(f"is_{tag}", 46080, 128, 1024, I32)
        P.wait("vector", ve, quant_cols)
        for tt in range(NT):
            for ch in range(8):
                zsl = zw[:, tt * H_FFN + ch * 1024: tt * H_FFN + (ch + 1) * 1024]
                vser(lambda e, z=zsl, tt=tt: e.scalar_tensor_tensor(ystg, z, stat[:, 440 + tt:441 + tt], z, OP.mult, OP.mult))
                vser(lambda e: e.tensor_copy(istg, ystg))
                vser(lambda e, z=zsl: e.tensor_copy(z, istg))
        zq_done = ve.n
        if ("ffz_" + tag) in taps:
            DOUT(f"dbg_ffz_{tag}", (128, 8 * H_FFN), BF16)
            DOUT(f"dbg_ffzs_{tag}", (128, 64))
            P.wait("gpsimd", ve, zq_done)
            dma(stg, "gpsimd", dbg[f"dbg_ffz_{tag}"][:, :], zw[:, :])
            dma(stg, "gpsimd", dbg[f"dbg_ffzs_{tag}"][:, :], stat[:, 400:464])
            tap_barrier()
        P.wait("sync", ve, zq_done)
        for tt in range(NT):
            dma(st, "sync", z_hbm[tt * 128:(tt + 1) * 128, :], zw[:, tt * H_FFN:(tt + 1) * H_FFN])
        P.wait("sync", st)
        for hc in range(64):
            dmaT(zw[:, hc * TOK:(hc + 1) * TOK], z_hbm[:, hc * 128:(hc + 1) * 128])
        zt_ready = tr.n
        A.free(f"wb_{tag}", f"ys_{tag}", f"is_{tag}")
        w2buf = A.carve(f"w2b_{tag}", 44032, 128, 2 * 512, BF16)
        xst = A.carve(f"xs_{tag}", 45056, 128, 512, F32)
        x2st = A.carve(f"x2_{tag}", 45568, 128, 512, F32)
        P.wait("tensor", tr, zt_ready)
        for ob in range(4):
            w2m = {}
            for hc in range(64):
                b2 = (hc % 2) * 512
                if hc >= 2 and (hc - 2) in w2m:
                    s, c = w2m[hc - 2]
                    P.wait("sync", s, c)
                dma(ld, "sync", w2buf[:, b2:b2 + 512], w2qT_g[hc * 128:(hc + 1) * 128, ob * 512:(ob + 1) * 512])
                ld2 = ld.n
                P.wait("tensor", ld, ld2)
                for tt in range(NT):
                    if hc == 0 and tt in bankmark:
                        s_, c_ = bankmark[tt]
                        P.wait("tensor", s_, c_)
                    fn = (lambda e, bk=banks[tt], l=zw[:, hc * TOK + tt * 128: hc * TOK + (tt + 1) * 128],
                          rr=w2buf[:, b2:b2 + 512], s=(hc == 0), p=(hc == 63):
                          e.matmul(bk[:, :], l, rr, start=s, stop=p))
                    if tt == NT - 1:
                        last_inc("tensor", pe, fn)
                    else:
                        top(fn)
                w2m[hc] = (pe, pe.n)
            mm_done = pe.n
            for tt in range(NT):
                P.wait("sync", ve)
                dma(ld, "sync", xst[:, :], src_dram[tt * 128:(tt + 1) * 128, ob * 512:(ob + 1) * 512])
                ldx = ld.n
                P.wait("vector", pe, mm_done)
                P.wait("vector", ld, ldx)
                P.wait("vector", st)
                vser(lambda e, bk=banks[tt], tt=tt: e.scalar_tensor_tensor(
                    x2st[:, 0:512], bk[:, :], stat[:, 448 + tt:449 + tt], xst[:, :], OP.mult, OP.add))
                bankmark[tt] = (ve, ve.n)
                P.wait("sync", ve)
                dma(st, "sync", dst_dram[tt * 128:(tt + 1) * 128, ob * 512:(ob + 1) * 512], x2st[:, 0:512])
            ev_done = (ve, ve.n)
            pass
            P.wait("tensor", ve)
        ffn_done = st.n
        A.free(f"z_{tag}", f"xnT_{tag}", f"w2b_{tag}", f"xs_{tag}", f"x2_{tag}")
        if ("ffn_" + tag) in taps:
            DOUT(f"dbg_ffn_{tag}", (TOK, D))
            P.wait("gpsimd", st, ffn_done)
            dma(stg, "gpsimd", dbg[f"dbg_ffn_{tag}"][:, :], dst_dram[:, :])
            tap_barrier()
        return ffn_done

    qtb_hbm = DRAM("qtb_hbm", (D, TOK), BF16)

    LORA = {}

    def ph_attn_b(src_dram, dst_dram):
        xnT = A.carve("xnTb", 32768, 128, NDT * TOK, BF16)
        ph_layernorm("ln_b", src_dram, xnT)
        xnT_ready = tr.n
        phase_barrier()
        wq_b, cq = cast_weight("bWq", D)
        wk_b, ck = cast_weight("bWk", D)
        wv_b, cv = cast_weight("bWv", D)
        wo_b, co = cast_weight("bWo", D)
        lstage = A.carve("lstage", 0, 128, 4 * D, BF16)
        lBT = A.carve("lBT", 4096, 128, 4 * NDT * 16, BF16)
        lAT = A.carve("lAT", 4608, 16, 4 * D, BF16)
        for pi, p in enumerate("qkvo"):
            if pi > 0:
                P.wait("gpsimd", pe)
            dma(cast, "gpsimd", lstage[0:RANK, 0:D], inp[f"bB{p}"][:, :])
            dma(cast, "gpsimd", lstage[0:128, D:D + 16 * NDT].rearrange("a (n r) -> a n r", n=NDT),
                inp[f"bA{p}"][:, :].rearrange("(n p) r -> p n r", p=128))
            ldb = cast.n
            P.wait("tensor", cast, ldb)
            for dt in range(NDT):
                if 7 in bankmark:
                    s_, c_ = bankmark[7]
                    P.wait("tensor", s_, c_)
                last_inc("tensor", pe, lambda e, dt=dt: e.transpose(bk16(7)[:, 0:16], lstage[0:RANK, dt * 128:(dt + 1) * 128], id_sb[0:RANK, 0:RANK]))
                P.wait("vector", pe)
                last_inc("vector", ve, lambda e, pi=pi, dt=dt: e.tensor_copy(lBT[:, (pi * NDT + dt) * 16:(pi * NDT + dt + 1) * 16], bk16(7)[:, 0:16]))
                bankmark[7] = (ve, ve.n)
                P.wait("tensor", ve)
                last_inc("tensor", pe, lambda e, dt=dt: e.transpose(bk16(7)[0:16, 0:128], lstage[0:128, D + dt * 16:D + (dt + 1) * 16], id_sb[0:128, 0:128]))
                P.wait("vector", pe)
                last_inc("vector", ve, lambda e, pi=pi, dt=dt: e.tensor_copy(lAT[0:16, pi * D + dt * 128:pi * D + (dt + 1) * 128], bk16(7)[0:16, 0:128]))
                bankmark[7] = (ve, ve.n)
                P.wait("tensor", ve)
        lxbT = A.carve("lxbT", 8704, 16, 4 * TOK, BF16)
        P.wait("tensor", tr, xnT_ready)
        for pi in range(4):
            for th in range(2):
                if 6 in bankmark:
                    s_, c_ = bankmark[6]
                    P.wait("tensor", s_, c_)
                for dt in range(NDT):
                    fn = (lambda e, pi=pi, dt=dt, th=th, s=(dt == 0), p=(dt == NDT - 1):
                          e.matmul(banks[6][0:RANK, :], lBT[:, (pi * NDT + dt) * 16:(pi * NDT + dt + 1) * 16],
                                   xnT[:, dt * TOK + th * 512:dt * TOK + (th + 1) * 512], start=s, stop=p))
                    if dt == NDT - 1:
                        last_inc("tensor", pe, fn)
                    else:
                        top(fn)
                P.wait("vector", pe)
                last_inc("vector", ve, lambda e, pi=pi, th=th: e.tensor_scalar(
                    lxbT[0:16, pi * TOK + th * 512:pi * TOK + (th + 1) * 512], banks[6][0:RANK, :], LORA_SC, None, OP.mult))
                bankmark[6] = (ve, ve.n)
        LORA["BT"], LORA["AT"], LORA["xbT"] = lBT, lAT, lxbT
        lora_ready = ve.n

        last_inc("vector", ve, lambda e: e.memset(misc[0:1, 580:612], 0.0))
        nrm0 = ve.n

        wts[0] = A.carve("wtb0", 16384, 128, NDT * 512, BF16)
        wts[1] = A.carve("wtb1", 20480, 128, NDT * 512, BF16)
        qstage = A.carve("qstageb", 24576, 128, 2 * 512, BF16)
        sqst = A.carve("sqst", 25088, 128, 512, BF16)

        def mk_evac_b(out_hbm, form, bias_base=None, brep_off=None, nrm_off=None):
            def evac(bank, bidx, oci, ot, th):
                i = evst["i"]; evst["i"] += 1
                sb = qstage[:, (i % 2) * 512:(i % 2) * 512 + 512]
                P.wait("vector", pe)
                P.wait("scalar", pe)
                if i - 2 in evst["marks"]:
                    s, c = evst["marks"][i - 2]
                    P.wait("vector", s, c)
                    P.wait("scalar", s, c)
                if bias_base is not None:
                    bc = bcols_sb[:, bias_base + oci * 4 + ot:bias_base + oci * 4 + ot + 1]
                    last_inc("scalar", se, lambda e, b=bank, o=sb, bc=bc: e.activation(out=o, in_=b[:, :], func=AF.Identity, bias=bc))
                    sem = se
                elif brep_off is not None:
                    last_inc("vector", ve, lambda e, b=bank, o=sb, oci=oci: e.scalar_tensor_tensor(o, b[:, :], 1.0, brep_sb[:, brep_off + oci * 512:brep_off + (oci + 1) * 512], OP.mult, OP.add))
                    sem = ve
                else:
                    last_inc("vector", ve, lambda e, b=bank, o=sb: e.tensor_copy(o, b[:, :]))
                    sem = ve
                bankmark[bidx] = (sem, sem.n)
                if nrm_off is not None:
                    hh = oci * 4 + ot
                    P.wait("vector", sem)
                    vser(lambda e, o=sb: e.tensor_tensor(sqst, o, o, OP.mult))
                    if 7 in bankmark:
                        s_, c_ = bankmark[7]
                        P.wait("tensor", s_, c_)
                    P.wait("tensor", ve)
                    last_inc("tensor", pe, lambda e: e.matmul(banks[7][0:1, 0:512], ones_sb[:, 0:1], sqst[:, :], start=True, stop=True))
                    P.wait("vector", pe)
                    vser(lambda e: e.reduce_max(misc[0:1, 612:613], banks[7][0:1, 0:512], axis=AX.X))
                    bankmark[7] = (ve, ve.n)
                    vser(lambda e, hh=hh, no=nrm_off: e.tensor_tensor(misc[0:1, no + hh:no + hh + 1], misc[0:1, no + hh:no + hh + 1], misc[0:1, 612:613], OP.max))
                P.wait("sync", sem)
                if form == "B":
                    o0 = oci * 512 + ot * 128
                    dma(st, "sync", out_hbm[o0:o0 + 128, th * 512:(th + 1) * 512], sb[0:128, :])
                else:
                    tt = ot * 2 + th
                    dma(st, "sync", out_hbm[tt * 128:(tt + 1) * 128, oci * 512:(oci + 1) * 512], sb[0:128, :])
                evst["marks"][i] = (st, st.n)
            return evac

        def lora_extra(pi):
            def extra(form, bank, oci, ot, th):
                P.wait("tensor", ve, lora_ready)
                if form == "B":
                    o0 = oci * 512 + ot * 128
                    fn = (lambda e, b=bank, pi=pi, o0=o0, th=th:
                          e.matmul(b[:, :], lAT[0:16, pi * D + o0:pi * D + o0 + 128],
                                   lxbT[0:16, pi * TOK + th * 512:pi * TOK + (th + 1) * 512], start=False, stop=True))
                else:
                    tt = ot * 2 + th
                    fn = (lambda e, b=bank, pi=pi, tt=tt, oci=oci:
                          e.matmul(b[:, :], lxbT[0:16, pi * TOK + tt * 128:pi * TOK + (tt + 1) * 128],
                                   lAT[0:16, pi * D + oci * 512:pi * D + (oci + 1) * 512], start=False, stop=True))
                last_inc("tensor", pe, fn)
            return extra

        P.wait("vector", ve, nrm0)
        project(wk_b, D, ck, "B", mk_evac_b(ktb_in, "B", bias_base=NDT, nrm_off=596), xnT, xnT_ready, extra=lora_extra(1))
        project(wv_b, D, cv, "A", mk_evac_b(vb_in, "A", brep_off=0), xnT, xnT_ready, extra=lora_extra(2))
        project(wq_b, D, cq, "B", mk_evac_b(qtb_hbm, "B", bias_base=0, nrm_off=580), xnT, xnT_ready, extra=lora_extra(0))
        kv_done = st.n
        P.wait("sync", ve)
        dma(st, "sync", nrm_in[0:1, 0:32], misc[0:1, 580:612])
        P.wait("gpsimd", st)
        for s_t, d_t in ((ktb_in, ktb_g), (vb_in, vb_g), (nrm_in, nrm_g)):
            cc.bump(1)
            P.emit("gpsimd", lambda e, h=cc.h, s=s_t, d=d_t: e.collective_compute(
                "AllGather", mybir.AluOpType.bypass, replica_groups=[list(range(NCORES))],
                ins=[s.opt()], outs=[d.opt()]).then_inc(h, 1))
        kvcc = cc.n
        if "bproj" in taps:
            DOUT("dbg_ktb", (D, TOK), BF16)
            DOUT("dbg_vbin", (TOK, D), BF16)
            DOUT("dbg_qtb", (D, TOK), BF16)
            DOUT("dbg_lx", (16, 4 * TOK), BF16)
            DOUT("dbg_nrm", (1, 64))
            P.wait("gpsimd", st, kv_done)
            P.wait("gpsimd", ve)
            dma(stg, "gpsimd", dbg["dbg_ktb"][:, :], ktb_in[:, :])
            dma(stg, "gpsimd", dbg["dbg_vbin"][:, :], vb_in[:, :])
            dma(stg, "gpsimd", dbg["dbg_qtb"][:, :], qtb_hbm[:, :])
            dma(stg, "gpsimd", dbg["dbg_lx"][:, :], lxbT)
            dma(stg, "gpsimd", dbg["dbg_nrm"][0:1, 0:32], misc[0:1, 580:612])
            tap_barrier()
        return kvcc, co

    def ph_attn_b_core(kvcc):
        A.free("lstage", "wtb0", "wtb1", "qstageb", "sqst", "xnTb")
        kbufs = [A.carve("kbA", 0, 128, H_FFN, BF16), A.carve("kbB", 10752, 128, H_FFN, BF16)]
        vbufs = [A.carve("vbA", 14848, 128, 64 * 129, BF16), A.carve("vbB", 18976, 128, 64 * 129, BF16)]
        ptb = A.carve("ptb", 23104, 128, 2 * 1024, BF16)
        qtr = A.carve("qtr", 24128, 128, 2 * 1024, BF16)
        aostg = A.carve("aostg", 25152, 128, 4 * 128, BF16)
        cb = A.carve("cbias", 25472, 128, 16, F32)
        P.wait("sync", cc, kvcc)
        vser(lambda e: e.tensor_copy(misc[0:1, 448:480], misc[0:1, 580:612]))
        for r in range(NCORES):
            P.wait("sync", ve)
            dma(ld, "sync", misc[0:1, 480:512], nrm_g[r:r + 1, :])
            P.wait("vector", ld)
            vser(lambda e: e.tensor_tensor(misc[0:1, 448:480], misc[0:1, 448:480], misc[0:1, 480:512], OP.max))
        vser(lambda e: e.tensor_tensor(misc[0:1, 512:528], misc[0:1, 448:464], misc[0:1, 464:480], OP.mult))
        P.wait("scalar", ve)
        last_inc("scalar", se, lambda e: e.activation(out=misc[0:1, 528:544], in_=misc[0:1, 512:528], func=AF.Sqrt))
        P.wait("vector", se)
        vser(lambda e: e.tensor_scalar(misc[0:1, 544:560], misc[0:1, 528:544], -SCALE_B, None, OP.mult))
        P.wait("tensor", ve)
        if 3 in bankmark:
            s_, c_ = bankmark[3]
            P.wait("tensor", s_, c_)
        last_inc("tensor", pe, lambda e: e.matmul(banks[3][:, 0:16], onesf[0:1, 0:128], misc[0:1, 544:560], start=True, stop=True))
        P.wait("vector", pe)
        vser(lambda e: e.tensor_copy(cb, banks[3][:, 0:16]))
        bankmark[3] = (ve, ve.n)
        for vb in vbufs:
            last_inc("vector", ve, lambda e, vb=vb: e.memset(vb[:, 128:64 * 129:129], 1.0))
        cb_ready = ve.n

        ldm = {}
        ptm = {}
        sbm = {}
        for h in range(NH_B):
            hb = h % 2
            if h - 2 in ldm:
                s_, c_ = ldm[h - 2]
                P.wait("sync", s_, c_)
            for rk in range(NCORES):
                dma(ld, "sync", kbufs[hb][:, rk * 1024:(rk + 1) * 1024],
                    ktb_g[rk * D + h * 128:rk * D + (h + 1) * 128, :])
            dma(ld, "sync", qtr[:, hb * 1024:(hb + 1) * 1024], qtb_hbm[h * 128:(h + 1) * 128, :])
            for kb in range(64):
                rk, lt = kb // 8, kb % 8
                dma(ld, "sync", vbufs[hb][:, kb * 129:kb * 129 + 128],
                    vb_g[rk * TOK + lt * 128:rk * TOK + (lt + 1) * 128, h * 128:(h + 1) * 128])
            ld_done = ld.n
            for kb in range(64):
                sb2 = kb % 2
                spsum = pfull[:, (4 + 2 * sb2) * 512:(4 + 2 * sb2) * 512 + 1024]
                gk = h * 64 + kb
                P.wait("tensor", ld, ld_done)
                if gk == 0:
                    P.wait("tensor", ve, cb_ready)
                if gk - 2 >= 0 and (gk - 2) in sbm:
                    s_, c_ = sbm[gk - 2]
                    P.wait("tensor", s_, c_)
                for th in range(2):
                    fn = (lambda e, sp=spsum, th=th, hb=hb, kb=kb:
                          e.matmul(sp[:, th * 512:(th + 1) * 512], kbufs[hb][:, kb * 128:(kb + 1) * 128],
                                   qtr[:, hb * 1024 + th * 512:hb * 1024 + (th + 1) * 512], start=True, stop=True))
                    last_inc("tensor", pe, fn)
                sc_done = pe.n
                P.wait("scalar", pe, sc_done)
                if gk - 2 >= 0 and (gk - 2) in ptm:
                    s_, c_ = ptm[gk - 2]
                    P.wait("scalar", s_, c_)
                P.wait("scalar", ve, cb_ready)
                last_inc("scalar", se, lambda e, sp=spsum, sb2=sb2, h=h:
                         e.activation(out=ptb[:, sb2 * 1024:(sb2 + 1) * 1024], in_=sp, func=AF.Exp,
                                      scale=SCALE_B, bias=cb[:, h:h + 1]))
                sbm[gk] = (se, se.n)
                P.wait("tensor", se)
                for qs in range(8):
                    acc = pfull[:, (qs // 3) * 512 + (qs % 3) * 129:(qs // 3) * 512 + (qs % 3) * 129 + 129]
                    if kb == 0 and ("acc", qs) in bankmark:
                        s_, c_ = bankmark[("acc", qs)]
                        P.wait("tensor", s_, c_)
                    fn = (lambda e, acc=acc, sb2=sb2, qs=qs, hb=hb, kb=kb,
                          s=(kb == 0 and qs in (0, 3, 6)), p=(kb == 63):
                          e.matmul(acc, ptb[:, sb2 * 1024 + qs * 128:sb2 * 1024 + (qs + 1) * 128],
                                   vbufs[hb][:, kb * 129:(kb + 1) * 129], start=s, stop=p))
                    if qs == 7:
                        last_inc("tensor", pe, fn)
                    else:
                        top(fn)
                ptm[gk] = (pe, pe.n)
            pv_done = pe.n
            for qs in range(8):
                acc = pfull[:, (qs // 3) * 512 + (qs % 3) * 129:(qs // 3) * 512 + (qs % 3) * 129 + 129]
                P.wait("vector", pe, pv_done)
                vser(lambda e, acc=acc, qs=qs: e.reciprocal(stat[:, 600 + qs:601 + qs], acc[:, 128:129]))
                st_slot = aostg[:, (qs % 4) * 128:(qs % 4) * 128 + 128]
                if qs >= 4:
                    P.wait("vector", st)
                vser(lambda e, acc=acc, o=st_slot, qs=qs: e.tensor_scalar(o, acc[:, 0:128], stat[:, 600 + qs:601 + qs], None, OP.mult))
                bankmark[("acc", qs)] = (ve, ve.n)
                P.wait("sync", ve)
                dma(st, "sync", ao_hbm[qs * 128:(qs + 1) * 128, h * 128:(h + 1) * 128], st_slot[0:128, :])
            ldm[h] = (ve, ve.n)
        attn_done = st.n
        if "bdiag" in taps:
            DOUT("dbg_bdiag", (128, 200))
            DOUT("dbg_vb", (128, 516), BF16)
            P.wait("gpsimd", st, attn_done)
            dma(stg, "gpsimd", dbg["dbg_bdiag"][0:1, 0:120], misc[0:1, 440:560])
            dma(stg, "gpsimd", dbg["dbg_bdiag"][:, 120:136], cb)
            dma(stg, "gpsimd", dbg["dbg_vb"][:, :], vbufs[0][:, 0:516])
            tap_barrier()
        A.free("kbA", "kbB", "vbA", "vbB", "ptb", "qtr", "aostg", "cbias")
        if "aob" in taps:
            DOUT("dbg_aob", (TOK, D), BF16)
            P.wait("gpsimd", st, attn_done)
            dma(stg, "gpsimd", dbg["dbg_aob"][:, :], ao_hbm[:, :])
            tap_barrier()
        return attn_done

    def ph_attn_b_oproj(attn_done, co, src_dram, dst_dram):
        phase_barrier()
        aoT = A.carve("aoT", 32768, 128, NDT * TOK, BF16)
        P.wait("sync", st, attn_done)
        for dt in range(NDT):
            dmaT(aoT[:, dt * TOK:(dt + 1) * TOK], ao_hbm[:, dt * 128:(dt + 1) * 128])
        aoT_ready = tr.n
        pi = 3
        P.wait("tensor", tr, aoT_ready)
        for th in range(2):
            if 6 in bankmark:
                s_, c_ = bankmark[6]
                P.wait("tensor", s_, c_)
            for dt in range(NDT):
                fn = (lambda e, dt=dt, th=th, s=(dt == 0), p=(dt == NDT - 1):
                      e.matmul(banks[6][0:RANK, :], LORA["BT"][:, (pi * NDT + dt) * 16:(pi * NDT + dt + 1) * 16],
                               aoT[:, dt * TOK + th * 512:dt * TOK + (th + 1) * 512], start=s, stop=p))
                if dt == NDT - 1:
                    last_inc("tensor", pe, fn)
                else:
                    top(fn)
            P.wait("vector", pe)
            last_inc("vector", ve, lambda e, th=th: e.tensor_scalar(
                LORA["xbT"][0:16, pi * TOK + th * 512:pi * TOK + (th + 1) * 512], banks[6][0:RANK, :], LORA_SC, None, OP.mult))
            bankmark[6] = (ve, ve.n)
        lora_o_ready = ve.n

        wts[0] = A.carve("wtob0", 16384, 128, NDT * 512, BF16)
        wts[1] = A.carve("wtob1", 20480, 128, NDT * 512, BF16)
        xst3 = A.carve("xst3", 24576, 128, 512, F32)
        x3st = A.carve("x3st", 25088, 128, 512, F32)
        x3tmp = A.carve("x3tmp", 25600, 128, 512, F32)

        def extra_o(form, bank, oci, ot, th):
            P.wait("tensor", ve, lora_o_ready)
            tt = ot * 2 + th
            last_inc("tensor", pe, lambda e, b=bank, tt=tt, oci=oci:
                     e.matmul(b[:, :], LORA["xbT"][0:16, pi * TOK + tt * 128:pi * TOK + (tt + 1) * 128],
                              LORA["AT"][0:16, pi * D + oci * 512:pi * D + (oci + 1) * 512], start=False, stop=True))

        def evac_o(bank, bidx, oci, ot, th):
            tt = ot * 2 + th
            P.wait("sync", ve)
            dma(ld, "sync", xst3[:, :], src_dram[tt * 128:(tt + 1) * 128, oci * 512:(oci + 1) * 512])
            ldx = ld.n
            P.wait("vector", pe)
            P.wait("vector", ld, ldx)
            P.wait("vector", st)
            vser(lambda e, b=bank, oci=oci: e.scalar_tensor_tensor(x3tmp[:, :], b[:, :], 1.0,
                 brep_sb[:, D + oci * 512:D + (oci + 1) * 512], OP.mult, OP.add))
            vser(lambda e: e.tensor_tensor(x3st[:, :], x3tmp[:, :], xst3[:, :], OP.add))
            bankmark[bidx] = (ve, ve.n)
            P.wait("sync", ve)
            dma(st, "sync", dst_dram[tt * 128:(tt + 1) * 128, oci * 512:(oci + 1) * 512], x3st[0:128, :])

        project(wcast_hbm["bWo"], D, co, "A", evac_o, aoT, aoT_ready, extra=extra_o)
        x3_done = st.n
        A.free("aoT", "wtob0", "wtob1", "xst3", "x3st", "x3tmp", "lAT", "lxbT", "lBT")
        if "x3" in taps:
            DOUT("dbg_x3", (TOK, D))
            P.wait("gpsimd", st, x3_done)
            dma(stg, "gpsimd", dbg["dbg_x3"][:, :], dst_dram[:, :])
            tap_barrier()
        return x3_done

    xnT1, xnT1_ready = ph_attn_a()
    if upto >= 2:
        scan_done = ph_attn_a_gates_scan(xnT1, xnT1_ready)
    A.free("xnT")
    if upto >= 3:
        x1_done = ph_attn_a_out(scan_done)
    if upto >= 4:
        ffn_cc = ph_ffn_quantprep()
        ffn1_done = ph_ffn("f1", xspA, xspB, ffn_cc)
    if upto >= 5:
        kvcc_b, co_b = ph_attn_b(xspB, xspA)
        attn_done_b = ph_attn_b_core(kvcc_b)
        x3_done = ph_attn_b_oproj(attn_done_b, co_b, xspB, xspA)
    if upto >= 6:
        ph_ffn("f2", xspA, y_out, ffn_cc, final=True)

    if upto >= 6:
        ysrc = None
    elif upto >= 5:
        ysrc = xspA
    elif upto >= 4:
        ysrc = xspB
    elif upto >= 3:
        ysrc = xspA
    else:
        ysrc = inp["x0"]
    A.free_all()
    if ysrc is not None:
        yroll = A.carve("yroll", 0, 128, 2 * D, F32)
        for t in range(NT):
            b = (t % 2) * D
            P.wait("sync", st)
            dma(ld, "sync", yroll[:, b:b + D], ysrc[t * 128:(t + 1) * 128, :])
            P.wait("sync", ld)
            dma(st, "sync", y_out[t * 128:(t + 1) * 128, :], yroll[:, b:b + D])
    P.wait("sync", st)
    P.wait("gpsimd", stg)
    P.wait("gpsimd", cc)

    P.finalize()
    return nc, es


def _prep_inputs(kw):
    f32 = np.float32
    x = np.asarray(kw["L0_attn_a_x"], f32).reshape(S_FULL, D)
    ident = np.eye(128, dtype=ml_dtypes.bfloat16)
    mask = np.where(np.arange(128)[:, None] <= np.arange(128)[None, :], 0.0, MASK_NEG).astype(f32)
    gamma_rep = np.broadcast_to(np.asarray(kw["L0_attn_a_gamma"], f32).reshape(1, NH_A * DV_A), (128, NH_A * DV_A)).copy()
    bq_cols = np.ascontiguousarray(np.asarray(kw["L0_attn_b_bq"], f32).reshape(NDT, 128).T)
    bk_cols = np.ascontiguousarray(np.asarray(kw["L0_attn_b_bk"], f32).reshape(NDT, 128).T)
    bv_rep = np.broadcast_to(np.asarray(kw["L0_attn_b_bv"], f32)[None, :], (128, D)).copy()
    bo_rep = np.broadcast_to(np.asarray(kw["L0_attn_b_bo"], f32)[None, :], (128, D)).copy()
    cfc = np.asarray(kw["L0_ffn_c_fc"], f32)
    cpr = np.asarray(kw["L0_ffn_c_proj"], f32)
    common = {
        "aWq": np.asarray(kw["L0_attn_a_Wq"], f32), "aWk": np.asarray(kw["L0_attn_a_Wk"], f32),
        "aWv": np.asarray(kw["L0_attn_a_Wv"], f32), "aWog": np.asarray(kw["L0_attn_a_Wog"], f32),
        "aWig": np.asarray(kw["L0_attn_a_Wig"], f32), "aWfg": np.asarray(kw["L0_attn_a_Wfg"], f32),
        "a_big": np.asarray(kw["L0_attn_a_big"], f32).reshape(NH_A, 1),
        "a_bfg": np.asarray(kw["L0_attn_a_bfg"], f32).reshape(NH_A, 1),
        "a_gamma_rep": gamma_rep,
        "aWout": np.asarray(kw["L0_attn_a_Wout"], f32),
        "b_bq_cols": bq_cols, "b_bk_cols": bk_cols, "b_bv_rep": bv_rep, "b_bo_rep": bo_rep,
        "identity": ident, "mask": mask,
    }
    for p in "qkvo":
        common[f"bW{p}"] = np.asarray(kw[f"L0_attn_b_W{p}"], f32)
        common[f"bB{p}"] = np.asarray(kw[f"L0_attn_b_B{p}"], f32)
        common[f"bA{p}"] = np.asarray(kw[f"L0_attn_b_A{p}"], f32)
    in_maps = []
    for j in range(NCORES):
        m = dict(common)
        m["x0"] = np.ascontiguousarray(x[j * TOK:(j + 1) * TOK, :])
        m["cfc_sl"] = np.ascontiguousarray(cfc[:, j * TOK:(j + 1) * TOK])
        m["cpr_sl"] = np.ascontiguousarray(cpr[:, j * TOK:(j + 1) * TOK])
        rm = np.zeros((NCORES, 2), f32)
        rm[:, 0] = np.where(np.arange(NCORES) < j, 0.0, NEG_BIG)
        rm[:, 1] = (np.arange(NCORES) < j).astype(f32)
        m["rankmask"] = rm
        in_maps.append(m)
    return in_maps


def kernel(**inputs):
    nc, es = build_program()
    in_maps = _prep_inputs(inputs)
    res = run_bass_kernel_spmd(nc, in_maps, core_ids=list(range(NCORES)))
    es.close()
    y = np.concatenate([np.asarray(r["y"], np.float32) for r in res.results], axis=0)
    return y.reshape(1, S_FULL, D)
